# version 05
# baseline (speedup 1.0000x reference)
import numpy as np
import ml_dtypes
import dataclasses
import concourse.bass as bass
import concourse.mybir as mybir
from concourse.bass_utils import run_bass_kernel_spmd
from contextlib import ExitStack

F32 = mybir.dt.float32
BF16 = mybir.dt.bfloat16
I32 = mybir.dt.int32
U8 = mybir.dt.uint8
AF = mybir.ActivationFunctionType
ALU = mybir.AluOpType
AX = mybir.AxisListType

T, C, H, D, HID = 2048, 1024, 16, 64, 4096
NT = T // 128
NJ = 4
PK = 1536
NSP = PK // 128
EPS = 1e-5
SCL = 0.125
MASKVAL = -30000.0
TF_SEQ = False
KB = 1024
VW = 65 * H

ENGS = ("sync", "gpsimd", "tensor", "vector", "scalar")
DMA_ENGS = ("sync", "gpsimd")
SY, GP, TE, V, SC = ENGS


class Op:
    __slots__ = ("gidx", "eng", "fn", "deps", "flagged", "inc_val", "tag", "eseq",
                 "need_nop")

    def __init__(self, gidx, eng, fn, deps, tag, eseq):
        self.gidx, self.eng, self.fn, self.deps, self.tag = gidx, eng, fn, deps, tag
        self.flagged = eng in DMA_ENGS
        self.inc_val = None
        self.eseq = eseq
        self.need_nop = False


class Sched:
    def __init__(self):
        self.ops = []

    def add(self, eng, fn, deps=(), tag=""):
        if not hasattr(self, "eseq_cnt"):
            self.eseq_cnt = {e: 0 for e in ENGS}
        deps = [d for d in deps if d is not None]
        eseq = self.eseq_cnt[eng]
        self.eseq_cnt[eng] += 1
        op = Op(len(self.ops), eng, fn, deps, tag, eseq)
        for d in deps:
            assert d.gidx < len(self.ops), f"forward dep at {tag}"
            if not (d.eng == eng == TE):
                d.flagged = True
        self.ops.append(op)
        return op

    def finalize(self):
        cnt = {e: 0 for e in ENGS}
        for op in self.ops:
            if op.flagged:
                cnt[op.eng] += 16 if op.eng in DMA_ENGS else 1
                op.inc_val = cnt[op.eng]

    def emit_engine(self, eng_obj, eng_name, sems):
        for op in self.ops:
            if op.eng != eng_name:
                continue
            waits = {}
            for d in op.deps:
                if d.eng == eng_name and eng_name == TE:
                    continue
                waits[d.eng] = max(waits.get(d.eng, 0), d.inc_val)
            for e in ENGS:
                if e in waits:
                    eng_obj.wait_ge(sems[e], waits[e])
            inst = op.fn(eng_obj)
            if op.flagged:
                assert inst is not None, f"flagged op {op.tag} returned no inst"
                inst.then_inc(sems[op.eng], 16 if op.eng in DMA_ENGS else 1)


def strided2(ap2d, start_el, stride_el, n1, n2):
    base = ap2d[:, start_el:start_el + 1]
    return dataclasses.replace(base, ap=[list(base.ap[0]), [stride_el, n1], [1, n2]])


def build_nc(debug=False):
    nc = bass.Bass("TRN2", target_bir_lowering=False)
    s = Sched()

    X = nc.dram_tensor("x", [T, C], F32, kind="ExternalInput").ap()
    WQ = nc.dram_tensor("wq", [C, C], F32, kind="ExternalInput").ap()
    WK = nc.dram_tensor("wk", [C, C], F32, kind="ExternalInput").ap()
    WV = nc.dram_tensor("wv", [C, C], F32, kind="ExternalInput").ap()
    WP = nc.dram_tensor("wp", [C, C], F32, kind="ExternalInput").ap()
    BQ = nc.dram_tensor("bq", [C], F32, kind="ExternalInput").ap()
    BK = nc.dram_tensor("bk", [C], F32, kind="ExternalInput").ap()
    BV = nc.dram_tensor("bv", [C], F32, kind="ExternalInput").ap()
    BP = nc.dram_tensor("bp", [C], F32, kind="ExternalInput").ap()
    W1 = nc.dram_tensor("w1", [HID, C], F32, kind="ExternalInput").ap()
    B1 = nc.dram_tensor("b1", [HID], F32, kind="ExternalInput").ap()
    W2 = nc.dram_tensor("w2", [C, HID], F32, kind="ExternalInput").ap()
    B2 = nc.dram_tensor("b2", [C], F32, kind="ExternalInput").ap()
    IDB = nc.dram_tensor("identb", [128, 128], BF16, kind="ExternalInput").ap()
    IDF = nc.dram_tensor("identf", [128, 128], F32, kind="ExternalInput").ap()
    MSK = nc.dram_tensor("masks", [128, 3 * 512], BF16, kind="ExternalInput").ap()
    OUT = nc.dram_tensor("out", [T, C], F32, kind="ExternalOutput").ap()
    BW1T = nc.dram_tensor("bw1t", [C, HID], BF16, kind="Internal").ap()
    BW2T = nc.dram_tensor("bw2t", [HID, C], BF16, kind="Internal").ap()
    A2S = nc.dram_tensor("a2s", [C], F32, kind="Internal").ap()

    stack = ExitStack()
    rem = nc.sbuf_bytes_remaining
    ARENA = (rem - 64) // 32 * 32
    assert ARENA >= 212480, f"arena only {ARENA}"
    arena = stack.enter_context(nc.sbuf_tensor("arena", [128, ARENA], U8))

    _bump = [0]

    def small(shape, dtype):
        es = mybir.dt.size(dtype)
        n = 1
        for d in shape[1:]:
            n *= d
        off = _bump[0]
        _bump[0] = (off + n * es + 31) // 32 * 32
        return view(off, shape, dtype)

    def view(off_b, shape, dtype):
        es = mybir.dt.size(dtype)
        n = 1
        for d in shape[1:]:
            n *= d
        assert off_b % 32 == 0 and off_b + n * es <= ARENA, (off_b, shape)
        ap = arena.ap()[:, off_b:off_b + n * es].bitcast(dtype)
        if len(shape) > 2:
            pat = "p (" + " ".join(f"d{i}" for i in range(1, len(shape))) + ") -> p " + \
                  " ".join(f"d{i}" for i in range(1, len(shape)))
            kw = {f"d{i}": shape[i] for i in range(1, len(shape))}
            ap = ap.rearrange(pat, **kw)
        if shape[0] != 128:
            ap = ap[0:shape[0]]
        return ap

    identb = small([128, 128], BF16)
    identf = small([128, 128], F32)
    masks = small([128, 3 * 512], BF16)
    onesb = small([128, 128], BF16)
    onesrow = small([1, 128], F32)
    bq_pm = small([128, 8], F32)
    bk_pm = small([128, 8], F32)
    bv_pm = small([128, 8], F32)
    b1_pm = small([128, 32], F32)
    a1_pm = small([128, 32], F32)
    a2_pm = small([128, 10], F32)
    stats = small([128, 20], F32)
    stats2 = small([128, 32], F32)
    bnat = small([32, 128], F32)
    brow = small([1, 1024], F32)
    a2row = small([1, 1024], F32)
    bpb = small([128, 1024], F32)
    SM = 18944
    assert _bump[0] <= SM, _bump[0]

    def kb2(x):
        return int(x * KB)

    QT = view(SM, [128, 8 * T], BF16)
    KT = view(SM + 32 * KB, [128, 8 * PK], BF16)
    VPOFF = SM + 56 * KB
    VP = view(VPOFF, [128, NSP * VW], BF16)
    wpbf = view(kb2(100.5), [128, 8 * 1024], BF16)
    X2OFF = kb2(143.5)
    x2 = view(X2OFF, [128, NT * C], F32)

    def x2t(i):
        return view(X2OFF + 4 * KB * i, [128, C], F32)

    XnT = view(kb2(116.5), [128, 8 * T], BF16)
    wbf = view(kb2(148.5), [128, 8 * 1024], BF16)
    wbfB = view(kb2(190.5), [128, 8 * 1024], BF16)
    wf32 = view(kb2(164.5), [128, 2 * 1024], F32)
    VTp = view(kb2(172.5), [128, 2 * 384], BF16)
    xin = view(kb2(174.5), [128, 2 * 1024], F32)
    xnbf = view(kb2(182.5), [128, 2 * 1024], BF16)
    dmp0 = view(kb2(186.5), [128, 1024], F32)
    w1nat = view(kb2(174.5), [128, 2 * 1024], F32)
    w1s = view(kb2(182.5), [128, 2 * 1024], BF16)
    wstage = view(kb2(186.5), [128, 2 * 1024], BF16)
    PT = view(kb2(116.5), [128, NSP * 512], BF16)
    dmp3 = view(kb2(116.5), [128, 1024], F32)
    YT = view(kb2(128.5), [128, 8 * 512], BF16)
    bcr = view(kb2(136.5), [128, 512], F32)
    scrA = view(kb2(138.5), [128, 1024], F32)
    rdF = view(kb2(138.5), [128, 512], F32)
    rdrow = view(kb2(140.5), [128, 512], BF16)
    tmpY = view(kb2(142.5), [64, 512], BF16)
    bw1q = view(SM, [128, 2 * 8 * 1024], BF16)
    bw2q = view(SM + 32 * KB, [128, 2 * 8 * 1024], BF16)
    Xn2T = view(kb2(82.5), [128, 8 * 512], BF16)
    a2b = view(kb2(90.5), [128, 1024], F32)
    hTq = view(kb2(94.5), [128, 8 * 512], BF16)
    xn2bf = view(kb2(102.5), [128, 1024], BF16)
    b2b = view(kb2(104.5), [128, 1024], F32)
    outbuf = view(kb2(108.5), [128, 2 * 1024], F32)
    acc = view(kb2(116.5), [128, 4 * 1024], F32)
    scr4 = view(kb2(132.5), [128, 1024], F32)
    dmp4 = view(kb2(136.5), [128, 1024], F32)

    p01 = ExitStack()
    ptp = p01.enter_context(nc.psum_tensor("ptp", [128, 2048], BF16))
    psq = p01.enter_context(nc.psum_tensor("psq", [128, 2048], F32))
    ptw = p01.enter_context(nc.psum_tensor("ptw", [128, 2048], BF16))
    ptpF = ptp.ap().bitcast(F32)
    ptwF = ptw.ap().bitcast(F32)

    dI1 = s.add(SY, lambda e: e.dma_start(out=identb, in_=IDB), tag="ld_identb")

    def _sep(dep, tag):
        return s.add(V, lambda v: v.tensor_copy(stats[0:1, 18:19],
                                                identb[0:1, 0:1]),
                     deps=[dep], tag=tag)

    sp1 = _sep(dI1, "sep1")
    dI2 = s.add(SY, lambda e: e.dma_start(out=identf, in_=IDF), deps=[sp1],
                tag="ld_identf")
    sp2 = _sep(dI2, "sep2")
    dMs = s.add(SY, lambda e: e.dma_start(out=masks, in_=MSK), deps=[sp2],
                tag="ld_masks")
    sp3 = _sep(dMs, "sep3")
    mo1 = s.add(V, lambda v: v.memset(onesb, 1.0), tag="ones_b")
    mo2 = s.add(V, lambda v: v.memset(onesrow, 1.0), tag="ones_row")
    mz2 = s.add(V, lambda v: v.memset(strided2(VP, 64, 65, H * NSP, 1), 1.0),
                tag="vp_ones")
    vp_init = [mz2]

    def pm_load(name, src, n8, dst, col0, dep, extra):
        d = s.add(SY, lambda e: e.dma_start(
            out=bnat[0:n8, :], in_=src.rearrange("(a b) -> a b", b=128)),
            deps=[dep], tag=f"{name}_nat")
        m = s.add(TE, lambda te: te.matmul(
            ptpF[0:128, col0:col0 + n8], bnat[0:n8, :], identf[0:n8, 0:n8],
            start=True, stop=True), deps=[d, dI2] + extra, tag=f"{name}_mm")
        ev = s.add(V, lambda v: v.tensor_copy(dst[:, 0:n8], ptpF[:, col0:col0 + n8]),
                   deps=[m], tag=f"{name}_ev")
        return d, m, ev

    dq_, mq, eq = pm_load("bq", BQ, 8, bq_pm, 0, sp3, [])
    dk_, mk, ek = pm_load("bk", BK, 8, bk_pm, 8, mq, [mq])
    dv_, mv, ev_ = pm_load("bv", BV, 8, bv_pm, 16, mk, [mk])
    db1, mb1, eb1 = pm_load("b1", B1, 32, b1_pm, 32, mv, [mv])

    dbp = s.add(SY, lambda e: e.dma_start(out=brow, in_=BP[None, :]), deps=[db1],
                tag="bp_row")
    mbp = s.add(TE, lambda te: (te.matmul(ptpF[:, 0:512], onesrow, brow[:, 0:512],
                                          start=True, stop=True),
                                te.matmul(ptpF[:, 512:1024], onesrow, brow[:, 512:1024],
                                          start=True, stop=True))[-1],
                deps=[dbp, mo2, eq, ek, ev_, eb1], tag="bp_mm")
    ebp = s.add(V, lambda v: v.tensor_copy(bpb, ptpF[:, 0:1024]), deps=[mbp],
                tag="bp_ev")

    def ln_stats(src_ap, dump_ap, deps, tag):
        t1 = s.add(V, lambda v: v.bn_stats(stats[:, 0:6], src_ap[:, 0:512]),
                   deps=deps, tag=tag + "_bn0")
        t2 = s.add(V, lambda v: v.bn_stats(stats[:, 6:12], src_ap[:, 512:1024]),
                   deps=deps, tag=tag + "_bn1")
        t3 = s.add(V, lambda v: v.bn_aggr(stats[:, 12:14], stats[:, 0:12]),
                   deps=[t1, t2], tag=tag + "_ag")
        t4 = s.add(V, lambda v: v.tensor_scalar(
            stats[:, 14:15], stats[:, 13:14], 1.0, EPS, ALU.mult, ALU.add),
            deps=[t3], tag=tag + "_veps")
        t5 = s.add(SC, lambda e: e.activation(stats[:, 15:16], stats[:, 14:15], AF.Ln),
                   deps=[t4], tag=tag + "_ln")
        t6 = s.add(SC, lambda e: e.activation(stats[:, 16:17], stats[:, 15:16], AF.Exp,
                                              scale=-0.5), deps=[t5], tag=tag + "_ex")
        tb = s.add(V, lambda v: v.scalar_tensor_tensor(
            stats[:, 17:18], stats[:, 12:13], -1.0, stats[:, 16:17],
            ALU.mult, ALU.mult), deps=[t6, t4], tag=tag + "_bias")
        return tb

    xap = [None] * NT
    xtp = [None] * NT
    xev = [None] * NT
    sts = [None] * NT
    for i in range(NT):
        bi = i % 2
        ld = s.add(SY, lambda e, i=i, bi=bi: e.dma_start(
            out=xin[:, 1024 * bi:1024 * (bi + 1)], in_=X[128 * i:128 * (i + 1), :]),
            deps=[xap[i - 2] if i >= 2 else None, mbp if i == 0 else None,
                  sts[0] if i == 1 else None],
            tag=f"x_ld{i}")
        st = ln_stats(xin[:, 1024 * bi:1024 * (bi + 1)], dmp0, [ld, xap[i - 1]],
                      f"ln1_{i}")
        sts[i] = st
        xap[i] = s.add(SC, lambda e, bi=bi: e.activation(
            xnbf[:, 1024 * bi:1024 * (bi + 1)], xin[:, 1024 * bi:1024 * (bi + 1)],
            AF.Identity, bias=stats[:, 17:18], scale=stats[:, 16:17]),
            deps=[st, xtp[i - 2] if i >= 2 else None], tag=f"ln1_ap{i}")
        def _tp(te, bi=bi):
            for m in range(8):
                r = te.transpose(ptp[:, 1024 * bi + 128 * m:1024 * bi + 128 * (m + 1)],
                                 xnbf[:, 1024 * bi + 128 * m:1024 * bi + 128 * (m + 1)],
                                 identb)
            return r
        xtp[i] = s.add(TE, _tp, deps=[xap[i], dI1, xev[i - 2] if i >= 2 else None,
                                      ebp if i < 2 else None], tag=f"xnT_tp{i}")
        xev[i] = s.add(V, lambda v, i=i, bi=bi: v.tensor_copy(
            XnT.rearrange("p (f t) -> p f t", f=8)[:, :, 128 * i:128 * (i + 1)],
            ptp[:, 1024 * bi:1024 * (bi + 1)].rearrange("p (f c) -> p f c", f=8)),
            deps=[xtp[i]], tag=f"xnT_ev{i}")
    xnT_done = xev[NT - 1]
    dbg_tok = []
    if debug:
        DXN = nc.dram_tensor("dxnt", [128, 8 * T], BF16, kind="ExternalOutput").ap()
        dbg_tok.append(s.add(GP, lambda e: e.dma_start(out=DXN, in_=XnT),
                             deps=[xnT_done], tag="dump_xnt"))

    tf = {}
    tf_stage_war = [None, None]
    tfp = {}

    def transform_gen():
        for i in range(32):
            bi = i % 2
            dl = s.add(SY, lambda e, i=i, bi=bi: e.dma_start(
                out=w1nat[:, 1024 * bi:1024 * (bi + 1)],
                in_=W1[128 * i:128 * (i + 1), :]),
                deps=[xap[NT - 1] if i < 2 else None,
                      xtp[NT - 1] if i < 2 else None,
                      tfp.get(("sg", i - 2)), tfp.get(("ab", i - 2))],
                tag=f"w1_ld{i}")
            ab = s.add(V, lambda v, i=i, bi=bi: v.reduce_sum(
                a1_pm[:, i:i + 1], w1nat[:, 1024 * bi:1024 * (bi + 1)], axis=AX.X,
                apply_absolute_value=True), deps=[dl], tag=f"w1_abs{i}")
            tfp[("ab", i)] = ab
            sg = s.add(SC, lambda e, bi=bi: e.activation(
                w1s[:, 1024 * bi:1024 * (bi + 1)], w1nat[:, 1024 * bi:1024 * (bi + 1)],
                AF.Sign), deps=[dl, tfp.get(("tp", i - 2)),
                                xtp[NT - 1] if i < 2 else None], tag=f"w1_sg{i}")
            tfp[("sg", i)] = sg
            def _tpw(te, bi=bi):
                for m in range(8):
                    r = te.transpose(ptw[:, 1024 * bi + 128 * m:1024 * bi + 128 * (m + 1)],
                                     w1s[:, 1024 * bi + 128 * m:1024 * bi + 128 * (m + 1)],
                                     identb)
                return r
            tp = s.add(TE, _tpw, deps=[sg, tfp.get(("ev", i - 2)), dI1], tag=f"w1_tp{i}")
            tfp[("tp", i)] = tp
            evw = s.add(V, lambda v, bi=bi: v.tensor_copy(
                wstage[:, 1024 * bi:1024 * (bi + 1)], ptw[:, 1024 * bi:1024 * (bi + 1)]),
                deps=[tp, tf_stage_war[bi]], tag=f"w1_ev{i}")
            tfp[("ev", i)] = evw
            dma = s.add(GP, lambda e, i=i, bi=bi: e.dma_start(
                out=BW1T.rearrange("(m p) q -> p m q", p=128)[:, :, 128 * i:128 * (i + 1)],
                in_=wstage[:, 1024 * bi:1024 * (bi + 1)].rearrange("p (m c) -> p m c", m=8)),
                deps=[evw], tag=f"w1_st{i}")
            tf_stage_war[bi] = dma
            yield
        tf["w1_done"] = s.add(GP, lambda e: e.nop(),
                              deps=[tf_stage_war[0], tf_stage_war[1]], tag="w1_fin")
        tf["a1f"] = s.add(V, lambda v: v.tensor_scalar(
            a1_pm[:, 0:32], a1_pm[:, 0:32], 1.0 / C, None, ALU.mult),
            deps=[tfp[("ab", 31)]], tag="a1_scale")
        for it in range(32):
            d, cc = it // 4, it % 4
            bi = it % 2
            dl = s.add(SY, lambda e, d=d, cc=cc, bi=bi: e.dma_start(
                out=w1nat[:, 1024 * bi:1024 * (bi + 1)],
                in_=W2[128 * d:128 * (d + 1), 1024 * cc:1024 * (cc + 1)]),
                deps=[tfp.get(("sg", 31)) if it < 2 else None,
                      tfp.get(("ab", 31)) if it < 2 else None,
                      tfp.get(("sg2", it - 2)), tfp.get(("rd", it - 2))],
                tag=f"w2_ld{it}")
            rd = s.add(V, lambda v, bi=bi: v.reduce_sum(
                a2_pm[:, 8 + bi:9 + bi], w1nat[:, 1024 * bi:1024 * (bi + 1)], axis=AX.X,
                apply_absolute_value=True),
                deps=[dl, tfp.get(("acc", it - 2))], tag=f"w2_abs{it}")
            tfp[("rd", it)] = rd
            if cc == 0:
                acc_ = s.add(V, lambda v, d=d, bi=bi: v.tensor_copy(
                    a2_pm[:, d:d + 1], a2_pm[:, 8 + bi:9 + bi]),
                    deps=[rd, tfp.get(("acc", it - 1))], tag=f"w2_acc{it}")
            else:
                acc_ = s.add(V, lambda v, d=d, bi=bi: v.tensor_add(
                    a2_pm[:, d:d + 1], a2_pm[:, d:d + 1], a2_pm[:, 8 + bi:9 + bi]),
                    deps=[rd, tfp.get(("acc", it - 1))], tag=f"w2_acc{it}")
            tfp[("acc", it)] = acc_
            sg = s.add(SC, lambda e, bi=bi: e.activation(
                w1s[:, 1024 * bi:1024 * (bi + 1)], w1nat[:, 1024 * bi:1024 * (bi + 1)],
                AF.Sign), deps=[dl, tfp.get(("tp2", it - 2)),
                                tfp.get(("tp", 31)) if it < 2 else None],
                tag=f"w2_sg{it}")
            tfp[("sg2", it)] = sg
            def _tpw2(te, bi=bi):
                for m in range(8):
                    r = te.transpose(
                        ptw[:, 1024 * bi + 128 * m:1024 * bi + 128 * (m + 1)],
                        w1s[:, 1024 * bi + 128 * m:1024 * bi + 128 * (m + 1)], identb)
                return r
            tp = s.add(TE, _tpw2, deps=[sg, tfp.get(("ev2", it - 2)),
                                        tfp.get(("ev", 30)) if it < 2 else None,
                                        tfp.get(("ev", 31)) if it < 2 else None],
                       tag=f"w2_tp{it}")
            tfp[("tp2", it)] = tp
            evw = s.add(V, lambda v, bi=bi: v.tensor_copy(
                wstage[:, 1024 * bi:1024 * (bi + 1)], ptw[:, 1024 * bi:1024 * (bi + 1)]),
                deps=[tp, tf_stage_war[bi]], tag=f"w2_ev{it}")
            tfp[("ev2", it)] = evw
            dma = s.add(GP, lambda e, d=d, cc=cc, bi=bi: e.dma_start(
                out=BW2T.rearrange("(x m p) q -> x p m q", p=128, m=8)[
                    cc][:, :, 128 * d:128 * (d + 1)],
                in_=wstage[:, 1024 * bi:1024 * (bi + 1)].rearrange(
                    "p (m c) -> p m c", m=8)), deps=[evw], tag=f"w2_st{it}")
            tf_stage_war[bi] = dma
            yield
        tf["w2_done"] = s.add(GP, lambda e: e.nop(),
                              deps=[tf_stage_war[0], tf_stage_war[1]], tag="w2_fin")
        a2f = s.add(V, lambda v: v.tensor_scalar(
            a2_pm[:, 0:8], a2_pm[:, 0:8], 1.0 / HID, None, ALU.mult),
            deps=[tfp[("acc", 31)]], tag="a2_scale")
        m6 = s.add(TE, lambda te: te.matmul(ptwF[0:8, 0:128], a2_pm[:, 0:8], identf,
                                            start=True, stop=True),
                   deps=[a2f, tfp[("ev2", 30)], tfp[("ev2", 31)], dI2], tag="a2_tp")
        e6 = s.add(V, lambda v: v.tensor_copy(bnat[0:8, :], ptwF[0:8, 0:128]),
                   deps=[m6, eb1], tag="a2_ev")
        tf["e6"] = e6
        d7a = s.add(GP, lambda e: e.dma_start(out=A2S.rearrange("(a b) -> a b", b=128),
                                              in_=bnat[0:8, :]), deps=[e6],
                    tag="a2_out")
        tf["d7b"] = s.add(GP, lambda e: e.dma_start(out=a2row, in_=A2S[None, :]),
                          deps=[d7a], tag="a2_in")
        tf["sg_last"] = tfp[("sg2", 31)]
        tf["tp_last"] = tfp[("tp2", 31)]
        yield

    tgen = transform_gen()
    tf_exhausted = [False]

    def pump(n=1):
        if tf_exhausted[0]:
            return
        for _ in range(n):
            try:
                next(tgen)
            except StopIteration:
                tf_exhausted[0] = True
                return

    def load_cast_w(tagname, Wsrc, dst, war0):
        casts = []
        for k in range(8):
            bi = k % 2
            dl = s.add(SY, lambda e, k=k, bi=bi: e.dma_start(
                out=wf32[:, 1024 * bi:1024 * (bi + 1)],
                in_=Wsrc[128 * k:128 * (k + 1), :]),
                deps=[casts[k - 2] if k >= 2 else war0], tag=f"{tagname}_ld{k}")
            cv = s.add(V, lambda v, k=k, bi=bi: v.tensor_copy(
                dst[:, 1024 * k:1024 * (k + 1)], wf32[:, 1024 * bi:1024 * (bi + 1)]),
                deps=[dl], tag=f"{tagname}_cast{k}")
            casts.append(cv)
        return casts

    bank_war = [None] * 4
    vtp_war = [xev[NT - 1], xev[NT - 1]]
    vp_evs = {}

    def qkv_matmuls(name, kind, cast_done, bias_ev, bias_pm, wsrc):
        evs = []
        bank = 0
        for f in range(8):
            for j in range(NJ):
                bk = bank % 4
                bank += 1
                def _mm(te, f=f, j=j, bk=bk):
                    for k in range(8):
                        r = te.matmul(
                            psq[:, 512 * bk:512 * (bk + 1)],
                            wsrc[:, 1024 * k + 128 * f:1024 * k + 128 * (f + 1)],
                            XnT[:, T * k + 512 * j:T * k + 512 * (j + 1)],
                            start=(k == 0), stop=(k == 7))
                    return r
                mm = s.add(TE, _mm, deps=[cast_done, xnT_done, bank_war[bk]],
                           tag=f"{name}_mm{f}_{j}")
                if kind == "q":
                    ev = s.add(SC, lambda e, f=f, j=j, bk=bk: e.activation(
                        QT[:, T * f + 512 * j:T * f + 512 * (j + 1)],
                        psq[:, 512 * bk:512 * (bk + 1)], AF.Identity,
                        bias=bias_pm[:, f:f + 1]), deps=[mm, bias_ev],
                        tag=f"{name}_ev{f}_{j}")
                    bank_war[bk] = ev
                    evs.append(ev)
                elif kind == "k":
                    ev = s.add(SC, lambda e, f=f, j=j, bk=bk: e.activation(
                        KT[:, PK * f + 384 * j:PK * f + 384 * (j + 1)].rearrange(
                            "p (g c) -> p g c", c=3),
                        psq[:, 512 * bk:512 * (bk + 1)].rearrange(
                            "p (g c) -> p g c", c=4)[:, :, 0:3],
                        AF.Identity, bias=bias_pm[:, f:f + 1]), deps=[mm, bias_ev],
                        tag=f"{name}_ev{f}_{j}")
                    bank_war[bk] = ev
                    evs.append(ev)
                else:
                    it = f * NJ + j
                    vbi = it % 2
                    ev = s.add(SC, lambda e, bk=bk, vbi=vbi, f=f: e.activation(
                        VTp[:, 384 * vbi:384 * (vbi + 1)].rearrange(
                            "p (g c) -> p g c", c=3),
                        psq[:, 512 * bk:512 * (bk + 1)].rearrange(
                            "p (g c) -> p g c", c=4)[:, :, 0:3],
                        AF.Identity, bias=bias_pm[:, f:f + 1]),
                        deps=[mm, bias_ev, vp_evs.get(it - 2)], tag=f"{name}_ev{f}_{j}")
                    bank_war[bk] = ev
                    def _tpv(te, vbi=vbi):
                        for m in range(3):
                            r = te.transpose(
                                ptp[:, 1024 * vbi + 128 * m:1024 * vbi + 128 * (m + 1)],
                                VTp[:, 384 * vbi + 128 * m:384 * vbi + 128 * (m + 1)],
                                identb)
                        return r
                    tpv = s.add(TE, _tpv, deps=[ev, vtp_war[vbi]], tag=f"v_tp{f}_{j}")
                    def _vev(v, f=f, j=j, vbi=vbi):
                        dst = dataclasses.replace(
                            VP[:, 0:1],
                            offset=VP.offset + (3 * j) * VW + 130 * f,
                            ap=[list(VP.ap[0]), [VW, 3], [65, 2], [1, 64]])
                        src = ptp[:, 1024 * vbi:1024 * vbi + 384].rearrange(
                            "p (m c) -> p m c", m=3).rearrange(
                            "p m (h s) -> p m h s", s=64)
                        return v.tensor_copy(dst, src)
                    vev = s.add(V, _vev, deps=[tpv] + vp_init, tag=f"vp_ev{f}_{j}")
                    vtp_war[vbi] = vev
                    vp_evs[it] = vev
                    evs.append(vev)
                if not TF_SEQ:
                    pump(1)
        return evs

    casts_v = load_cast_w("wv", WV, wbf, None)
    evs_v = qkv_matmuls("wv", "v", casts_v[-1], ev_, bv_pm, wbf)
    v_done = evs_v[-1]
    vp_last = vp_evs[31]
    casts_k = load_cast_w("wk", WK, wbfB, evs_v[-1])
    evs_k = qkv_matmuls("wk", "k", casts_k[-1], ek, bk_pm, wbfB)
    k_done = evs_k[-1]
    casts_q = load_cast_w("wq", WQ, wbf, evs_k[-1])
    evs_q = qkv_matmuls("wq", "q", casts_q[-1], eq, bq_pm, wbf)
    q_done = evs_q[-1]
    casts_p = load_cast_w("wp", WP, wpbf, evs_q[-1])
    wp_done = casts_p[-1]
    pump(100)
    assert tf_exhausted[0] and "w2_done" in tf
    w1_done = tf["w1_done"]
    w2_done = tf["w2_done"]
    a1f = tf["a1f"]
    e6 = tf["e6"]
    d7b = tf["d7b"]

    if debug:
        DQT = nc.dram_tensor("dqt", [128, 8 * T], BF16, kind="ExternalOutput").ap()
        DKT = nc.dram_tensor("dkt", [128, 8 * PK], BF16, kind="ExternalOutput").ap()
        DVP = nc.dram_tensor("dvp", [128, NSP * VW], BF16, kind="ExternalOutput").ap()
        DA1 = nc.dram_tensor("da1", [128, 32], F32, kind="ExternalOutput").ap()
        dbg_tok.append(s.add(GP, lambda e: e.dma_start(out=DQT, in_=QT),
                             deps=[q_done], tag="dump_qt"))
        dbg_tok.append(s.add(GP, lambda e: e.dma_start(out=DKT, in_=KT),
                             deps=[k_done], tag="dump_kt"))
        dbg_tok.append(s.add(GP, lambda e: e.dma_start(out=DVP, in_=VP),
                             deps=[v_done, vp_init[0]], tag="dump_vp"))
        dbg_tok.append(s.add(GP, lambda e: e.dma_start(out=DA1, in_=a1_pm),
                             deps=[a1f], tag="dump_a1"))

    p01.close()
    p3ps = ExitStack()
    psS = p3ps.enter_context(nc.psum_tensor("psS", [128, 3 * 512], F32))
    psY = p3ps.enter_context(nc.psum_tensor("psY", [128, 2 * 512], F32))
    psBc = p3ps.enter_context(nc.psum_tensor("psBc", [128, 512], F32))
    psPt = p3ps.enter_context(nc.psum_tensor("psP", [128, 1024], F32))
    psP = psPt.ap()

    if debug:
        DAO = nc.dram_tensor("dao", [128, NT * 1024], F32, kind="ExternalOutput").ap()
        DYTA = nc.dram_tensor("dyta", [128, NJ * 4096], BF16, kind="ExternalOutput").ap()

    first_bank = [vp_last, q_done]
    dao_tok = None
    yta_tok = None
    x2_tok = {}
    xres_tok = {}
    pv_read = {}
    s_war = [None, None, None]
    y_war = [None, None]
    prev_bc = None
    prev_bce = None
    prev_yn = None
    prev_pmm = {}
    prev_stt = d7b
    prev_ap = None
    prev_res = None
    prev_rc = None
    ln2_tok = {}

    for j in range(NJ):
        nsp_j = 3 * (j + 1)
        y_dma = {}
        for h in range(H):
            f, par = h // 2, h % 2
            pv_prev = None
            pv_this = {}
            for spi in range(nsp_j):
                bk = spi % 3
                deps_mm = [k_done, q_done, s_war[bk]]
                if j == 0 and h == 0 and spi < 3:
                    deps_mm += first_bank + [dMs] + (dbg_tok[0:1] if debug else [])
                def _smm(te, f=f, par=par, spi=spi, j=j, bk=bk, bd=(spi >= 3 * j)):
                    off = 64 * par
                    r = te.matmul(
                        psS[:, 512 * bk:512 * (bk + 1)],
                        KT[off:off + 64, PK * f + 128 * spi:PK * f + 128 * (spi + 1)],
                        QT[off:off + 64, T * f + 512 * j:T * f + 512 * (j + 1)],
                        start=True, stop=True)
                    if bd:
                        kk = spi - 3 * j
                        r = te.matmul(psS[:, 512 * bk:512 * (bk + 1)], identb,
                                      masks[:, 512 * kk:512 * (kk + 1)],
                                      start=False, stop=True, skip_group_check=True)
                    return r
                mm = s.add(TE, _smm, deps=deps_mm, tag=f"smm{j}_{h}_{spi}")
                ex = s.add(SC, lambda e, spi=spi, bk=bk: e.activation(
                    PT[:, 512 * spi:512 * (spi + 1)], psS[:, 512 * bk:512 * (bk + 1)],
                    AF.Exp, scale=SCL),
                    deps=[mm, pv_read.get(spi), prev_stt if (h == 0 and j > 0) else None],
                    tag=f"exp{j}_{h}_{spi}")
                s_war[bk] = ex
                yb = h % 2
                pvdeps = [ex, v_done, pv_prev]
                if spi == 0:
                    pvdeps += [y_war[yb]]
                    if j == 0 and h < 2:
                        pvdeps += [q_done]
                def _pv(te, h=h, spi=spi, yb=yb, last=(spi == nsp_j - 1)):
                    lhs = VP[:, VW * spi + 65 * h:VW * spi + 65 * h + 65]
                    return te.matmul(psY[0:65, 512 * yb:512 * (yb + 1)], lhs,
                                     PT[:, 512 * spi:512 * (spi + 1)],
                                     start=(spi == 0), stop=last)
                pv_prev = s.add(TE, _pv, deps=pvdeps, tag=f"pv{j}_{h}_{spi}")
                pv_this[spi] = pv_prev
            for spi in range(nsp_j):
                pv_read[spi] = pv_this[spi]
            yb = h % 2
            rcl = s.add(SC, lambda e, yb=yb: e.activation(
                rdF[64:65, :], psY[64:65, 512 * yb:512 * (yb + 1)], AF.Ln),
                deps=[pv_prev, prev_rc, prev_res], tag=f"rcl{j}_{h}")
            def _rc(e):
                with nc.allow_low_precision("recip feeds a bf16 matmul operand"):
                    return e.activation(rdrow[64:65, :], rdF[64:65, :], AF.Exp,
                                        scale=-1.0)
            rc = s.add(SC, _rc, deps=[rcl, prev_bc], tag=f"rc{j}_{h}")
            prev_rc = rc
            bc = s.add(TE, lambda te: te.matmul(
                psBc[0:64, :], onesb[64:65, 0:64], rdrow[64:65, :],
                start=True, stop=True),
                deps=[rc, mo1, prev_bce,
                      q_done if (j == 0 and h == 0) else None],
                tag=f"bc{j}_{h}")
            prev_bc = bc
            bce = s.add(SC, lambda e: e.copy(bcr[0:64, :], psBc[0:64, :]),
                        deps=[bc, prev_yn], tag=f"bce{j}_{h}")
            prev_bce = bce
            if par == 0:
                yn = s.add(V, lambda v, f=f, yb=yb: v.tensor_mul(
                    YT[0:64, 512 * f:512 * (f + 1)],
                    psY[0:64, 512 * yb:512 * (yb + 1)], bcr[0:64, :]),
                    deps=[bce, prev_pmm.get(j - 1)], tag=f"yn{j}_{h}")
                prev_yn = yn
            else:
                ymul = s.add(V, lambda v, yb=yb: v.tensor_mul(
                    tmpY[0:64, :], psY[0:64, 512 * yb:512 * (yb + 1)], bcr[0:64, :]),
                    deps=[bce, y_dma.get(h - 2, prev_pmm.get(j - 1))],
                    tag=f"ynm{j}_{h}")
                yd = s.add(GP, lambda e, f=f: e.dma_start(
                    out=YT[64:128, 512 * f:512 * (f + 1)], in_=tmpY[0:64, :]),
                    deps=[ymul, prev_pmm.get(j - 1) if h == 1 else None],
                    tag=f"ydma{j}_{h}")
                y_dma[h] = yd
                prev_yn = ymul
            y_war[yb] = prev_yn
        if debug:
            yta_tok = s.add(GP, lambda e, j=j: e.dma_start(
                out=DYTA[:, 4096 * j:4096 * (j + 1)], in_=YT),
                deps=[prev_yn] + [y_dma[hh] for hh in y_dma], tag=f"dump_yta{j}")
        prev_aadd = None
        for q in range(4):
            i = 4 * j + q
            def _pmm(te, q=q):
                for f in range(8):
                    for nh in range(2):
                        r = te.matmul(
                            psP[:, 512 * nh:512 * (nh + 1)],
                            YT[:, 512 * f + 128 * q:512 * f + 128 * (q + 1)],
                            wpbf[:, 1024 * f + 512 * nh:1024 * f + 512 * (nh + 1)],
                            start=(f == 0), stop=(f == 7))
                return r
            pmdeps = [prev_yn, wp_done, prev_aadd,
                      e6 if (j == 0 and q == 0) else None,
                      yta_tok if debug else None] + \
                [y_dma[hh] for hh in y_dma]
            pmm = s.add(TE, _pmm, deps=pmdeps, tag=f"proj{j}_{q}")
            prev_pmm[j] = pmm
            tfdep = ([w2_done, tf["sg_last"], tf["tp_last"], d7b]
                     if 7 <= i <= 11 else [])
            xres_tok[i] = s.add(GP, lambda e, i=i: e.dma_start(
                out=x2t(i), in_=X[128 * i:128 * (i + 1), :]),
                deps=[q_done if i == 0 else None,
                      wp_done if i <= 8 else None] + tfdep, tag=f"xres{i}")
            aadd = s.add(V, lambda v: v.tensor_add(scrA, psP, bpb),
                         deps=[pmm, ebp, prev_bc, prev_ap,
                               x2_tok.get(i - 1)], tag=f"aout{j}_{q}")
            prev_aadd = aadd
            if debug:
                dao_tok = s.add(GP, lambda e, i=i: e.dma_start(
                    out=DAO[:, 1024 * i:1024 * (i + 1)], in_=scrA), deps=[aadd],
                    tag=f"dump_ao{i}")
            st = ln_stats(scrA, dmp3,
                          [aadd, prev_stt, pv_read.get(0),
                           dao_tok if debug else None], f"lnA_{i}")
            prev_stt = st
            ap2 = s.add(SC, lambda e: e.activation(
                scrA, scrA, AF.Identity, bias=stats[:, 17:18], scale=stats[:, 16:17]),
                deps=[st], tag=f"lnA_ap{i}")
            prev_ap = ap2
            res = s.add(V, lambda v, i=i: v.tensor_add(x2t(i), x2t(i), scrA),
                        deps=[ap2, xres_tok[i]], tag=f"resA{i}")
            x2_tok[i] = res
            prev_res = res

    if debug:
        DX2 = nc.dram_tensor("dx2", [128, NT * C], F32, kind="ExternalOutput").ap()
        DYT = nc.dram_tensor("dyt", [128, 8 * 512], BF16, kind="ExternalOutput").ap()
        DPT = nc.dram_tensor("dpt", [128, NSP * 512], BF16, kind="ExternalOutput").ap()
        dbg_tok.append(s.add(GP, lambda e: e.dma_start(out=DX2, in_=x2),
                             deps=[x2_tok[NT - 1]], tag="dump_x2"))
        dbg_tok.append(s.add(GP, lambda e: e.dma_start(out=DYT, in_=YT),
                             deps=[prev_pmm[NJ - 1]], tag="dump_yt"))
        dbg_tok.append(s.add(GP, lambda e: e.dma_start(out=DPT, in_=PT),
                             deps=[prev_pmm[NJ - 1]], tag="dump_pt"))

    for i in range(NT):
        tb = ln_stats(x2t(i), dmp3,
                      [x2_tok[i], prev_stt, prev_ap] + (dbg_tok[-2:] if debug else []),
                      f"ln2_{i}")
        c1 = s.add(V, lambda v, i=i: v.tensor_copy(stats2[:, i:i + 1], stats[:, 16:17]),
                   deps=[tb], tag=f"ln2_cr{i}")
        c2 = s.add(V, lambda v, i=i: v.tensor_copy(stats2[:, 16 + i:17 + i],
                                                   stats[:, 17:18]),
                   deps=[tb, c1], tag=f"ln2_cb{i}")
        prev_stt = c2
        ln2_tok[i] = c2

    p3ps.close()
    p4ps = ExitStack()
    psF = p4ps.enter_context(nc.psum_tensor("psF", [128, 2 * 512], F32))
    psO = p4ps.enter_context(nc.psum_tensor("psO", [128, 4 * 512], F32))
    ptp2 = p4ps.enter_context(nc.psum_tensor("ptp2", [128, 2 * 1024], BF16))
    ptp2F = ptp2.ap().bitcast(F32)

    p4f = [ln2_tok[NT - 1]] + dbg_tok
    db2 = s.add(SY, lambda e: e.dma_start(out=brow, in_=B2[None, :]),
                deps=[dbp, p4f[0]], tag="b2_row")
    mb2 = s.add(TE, lambda te: (te.matmul(ptp2F[:, 0:512], onesrow, brow[:, 0:512],
                                          start=True, stop=True),
                                te.matmul(ptp2F[:, 512:1024], onesrow,
                                          brow[:, 512:1024], start=True, stop=True))[-1],
                deps=[db2, mo2, p4f[0], e6], tag="b2_mm")
    eb2 = s.add(V, lambda v: v.tensor_copy(b2b, ptp2F[:, 0:1024]), deps=[mb2],
                tag="b2_ev")
    ma2 = s.add(TE, lambda te: (te.matmul(ptp2F[:, 0:512], onesrow, a2row[:, 0:512],
                                          start=True, stop=True),
                                te.matmul(ptp2F[:, 512:1024], onesrow,
                                          a2row[:, 512:1024], start=True, stop=True))[-1],
                deps=[d7b, eb2], tag="a2_mm")
    ea2 = s.add(V, lambda v: v.tensor_copy(a2b, ptp2F[:, 0:1024]), deps=[ma2],
                tag="a2_ev2")

    xn2_war = None
    xn2_tok = {}
    ge_hist = {}
    tp_war4 = [ea2, ea2]
    q_last_mm = {}
    f2_last = {}
    psO_war = [None] * 4
    dq1_war = {}
    dq2_war = {}
    acc_tok = {}
    out_tok = {}
    prev_ep = None
    prev_ap3 = None

    for j in range(NJ):
        evx = None
        for q in range(4):
            i = 4 * j + q
            tbi = i % 2
            ap_ = s.add(SC, lambda e, i=i: e.activation(
                xn2bf, x2t(i), AF.Identity, bias=stats2[:, 16 + i:17 + i],
                scale=stats2[:, i:i + 1]),
                deps=[ln2_tok[i], xn2_war], tag=f"ln2_ap{i}")
            def _tpx(te, tbi=tbi):
                for m in range(8):
                    r = te.transpose(
                        ptp2[:, 1024 * tbi + 128 * m:1024 * tbi + 128 * (m + 1)],
                        xn2bf[:, 128 * m:128 * (m + 1)], identb)
                return r
            tpx = s.add(TE, _tpx, deps=[ap_, tp_war4[tbi]], tag=f"xn2_tp{i}")
            xn2_war = tpx
            evx = s.add(V, lambda v, q=q, tbi=tbi: v.tensor_copy(
                Xn2T.rearrange("p (f t) -> p f t", f=8)[:, :, 128 * q:128 * (q + 1)],
                ptp2[:, 1024 * tbi:1024 * (tbi + 1)].rearrange("p (f c) -> p f c", f=8)),
                deps=[tpx, q_last_mm.get(j - 1)], tag=f"xn2_ev{i}")
            tp_war4[tbi] = evx
            xn2_tok[i] = evx
        xn2_done = evx
        for q4 in range(4):
            qbi = q4 % 2
            gidx = 4 * j + q4
            d1 = s.add(SY, lambda e, q4=q4, qbi=qbi: e.dma_start(
                out=bw1q[:, 8192 * qbi:8192 * (qbi + 1)].rearrange(
                    "p (k c) -> p k c", k=8),
                in_=BW1T.rearrange("(k p) q -> p k q", p=128)[
                    :, :, 1024 * q4:1024 * (q4 + 1)]),
                deps=([w1_done if gidx < 2 else None,
                      dq1_war.get(gidx - 2)] + (p4f if gidx < 2 else [])),
                tag=f"bw1_ld{j}_{q4}")
            d2 = s.add(SY, lambda e, q4=q4, qbi=qbi: e.dma_start(
                out=bw2q[:, 8192 * qbi:8192 * (qbi + 1)].rearrange(
                    "p (k c) -> p k c", k=8),
                in_=BW2T.rearrange("(x k p) q -> x p k q", p=128, k=8)[q4]),
                deps=[w2_done if gidx < 2 else None,
                      p4f[0] if gidx < 2 else None,
                      dq2_war.get(gidx - 2)], tag=f"bw2_ld{j}_{q4}")
            gel = {}
            for i8 in range(8):
                g = 8 * q4 + i8
                fb = i8 % 2
                def _f1(te, i8=i8, fb=fb, qbi=qbi):
                    for k in range(8):
                        r = te.matmul(
                            psF[:, 512 * fb:512 * (fb + 1)],
                            bw1q[:, 8192 * qbi + 1024 * k + 128 * i8:
                                 8192 * qbi + 1024 * k + 128 * (i8 + 1)],
                            Xn2T[:, 512 * k:512 * (k + 1)],
                            start=(k == 0), stop=(k == 7))
                    return r
                psf_war = gel.get(i8 - 2, ge_hist.get((gidx - 1, i8 + 6)))
                f1 = s.add(TE, _f1, deps=[d1, xn2_done, psf_war,
                                          prev_ep if (gidx == 0 and i8 < 2) else None],
                           tag=f"f1_{j}_{q4}_{i8}")
                ge = s.add(SC, lambda e, i8=i8, g=g, fb=fb: e.activation(
                    hTq[:, 512 * i8:512 * (i8 + 1)], psF[:, 512 * fb:512 * (fb + 1)],
                    AF.Gelu, bias=b1_pm[:, g:g + 1], scale=a1_pm[:, g:g + 1]),
                    deps=[f1, eb1, a1f, f2_last.get(gidx - 1)],
                    tag=f"ge_{j}_{q4}_{i8}")
                gel[i8] = ge
                ge_hist[(gidx, i8)] = ge
            last2 = None
            f2i = 0
            for q in range(4):
                for half in range(2):
                    sl = f2i % 4
                    f2i += 1
                    def _f2(te, q=q, half=half, sl=sl, qbi=qbi):
                        for k in range(8):
                            r = te.matmul(
                                psO[:, 512 * sl:512 * (sl + 1)],
                                hTq[:, 512 * k + 128 * q:512 * k + 128 * (q + 1)],
                                bw2q[:, 8192 * qbi + 1024 * k + 512 * half:
                                     8192 * qbi + 1024 * k + 512 * (half + 1)],
                                start=(k == 0), stop=(k == 7))
                        return r
                    f2 = s.add(TE, _f2, deps=[d2, gel[7], psO_war[sl],
                                              prev_ep if (gidx == 0 and f2i < 5)
                                              else None],
                               tag=f"f2_{j}_{q4}_{q}_{half}")
                    last2 = f2
                    if q4 == 0:
                        at = s.add(V, lambda v, q=q, half=half, sl=sl: v.tensor_copy(
                            acc[:, 1024 * q + 512 * half:1024 * q + 512 * (half + 1)],
                            psO[:, 512 * sl:512 * (sl + 1)]),
                            deps=[f2], tag=f"acc_{j}_{q4}_{q}_{half}")
                    else:
                        at = s.add(V, lambda v, q=q, half=half, sl=sl: v.tensor_add(
                            acc[:, 1024 * q + 512 * half:1024 * q + 512 * (half + 1)],
                            acc[:, 1024 * q + 512 * half:1024 * q + 512 * (half + 1)],
                            psO[:, 512 * sl:512 * (sl + 1)]),
                            deps=[f2, acc_tok[(j, q4 - 1, q, half)]],
                            tag=f"acc_{j}_{q4}_{q}_{half}")
                    psO_war[sl] = at
                    acc_tok[(j, q4, q, half)] = at
            f2_last[gidx] = last2
            q_last_mm[j] = last2
            dq1_war[gidx] = last2
            dq2_war[gidx] = last2
        for q in range(4):
            i = 4 * j + q
            obi = i % 2
            y1 = s.add(V, lambda v, q=q: v.tensor_mul(
                scr4, acc[:, 1024 * q:1024 * (q + 1)], a2b),
                deps=[acc_tok[(j, 3, q, 0)], acc_tok[(j, 3, q, 1)], ea2, prev_stt,
                      prev_ap3], tag=f"ep_a2_{i}")
            y2 = s.add(V, lambda v: v.tensor_add(scr4, scr4, b2b),
                       deps=[y1, eb2], tag=f"ep_b2_{i}")
            st = ln_stats(scr4, dmp4, [y2, prev_stt, prev_ap3], f"ln3_{i}")
            prev_stt = st
            ap3 = s.add(SC, lambda e, obi=obi: e.activation(
                outbuf[:, 1024 * obi:1024 * (obi + 1)], scr4, AF.Identity,
                bias=stats[:, 17:18], scale=stats[:, 16:17]),
                deps=[st, out_tok.get(i - 2)], tag=f"ln3_ap{i}")
            prev_ap3 = ap3
            res = s.add(V, lambda v, i=i, obi=obi: v.tensor_add(
                outbuf[:, 1024 * obi:1024 * (obi + 1)],
                outbuf[:, 1024 * obi:1024 * (obi + 1)], x2t(i)),
                deps=[ap3], tag=f"res3_{i}")
            prev_ep = res
            out_tok[i] = s.add(GP, lambda e, i=i, obi=obi: e.dma_start(
                out=OUT[128 * i:128 * (i + 1), :],
                in_=outbuf[:, 1024 * obi:1024 * (obi + 1)]),
                deps=[res], tag=f"out{i}")

    if debug:
        DACC = nc.dram_tensor("dacc", [128, 4 * 1024], F32, kind="ExternalOutput").ap()
        DXN2 = nc.dram_tensor("dxn2", [128, 8 * 512], BF16, kind="ExternalOutput").ap()
        DHT = nc.dram_tensor("dht", [128, 8 * 512], BF16, kind="ExternalOutput").ap()
        DA2B = nc.dram_tensor("da2b", [128, 1024], F32, kind="ExternalOutput").ap()
        DB2B = nc.dram_tensor("db2b", [128, 1024], F32, kind="ExternalOutput").ap()
        s.add(GP, lambda e: e.dma_start(out=DACC, in_=acc),
              deps=[acc_tok[(NJ - 1, 3, q, hh)] for q in range(4) for hh in range(2)],
              tag="dump_acc")
        s.add(GP, lambda e: e.dma_start(out=DXN2, in_=Xn2T),
              deps=[xn2_tok[4 * (NJ - 1) + 3]], tag="dump_xn2")
        s.add(GP, lambda e: e.dma_start(out=DHT, in_=hTq),
              deps=[f2_last[4 * (NJ - 1) + 3]], tag="dump_ht")
        s.add(GP, lambda e: e.dma_start(out=DA2B, in_=a2b), deps=[ea2], tag="dump_a2b")
        s.add(GP, lambda e: e.dma_start(out=DB2B, in_=b2b), deps=[eb2], tag="dump_b2b")

    s.add(GP, lambda e: e.nop(), deps=[out_tok[i] for i in range(NT)], tag="fin")

    s.finalize()
    semstack = ExitStack()
    sems = {e: semstack.enter_context(nc.semaphore(f"sem_{e}")) for e in ENGS}
    with nc.Block() as block:
        @block.sync
        def _(eng):
            s.emit_engine(eng, SY, sems)

        @block.gpsimd
        def _(eng):
            s.emit_engine(eng, GP, sems)

        @block.tensor
        def _(eng):
            s.emit_engine(eng, TE, sems)

        @block.vector
        def _(eng):
            s.emit_engine(eng, V, sems)

        @block.scalar
        def _(eng):
            s.emit_engine(eng, SC, sems)

    return nc


def _aux_arrays():
    identb = np.eye(128, dtype=ml_dtypes.bfloat16)
    identf = np.eye(128, dtype=np.float32)
    masks = np.zeros((128, 3 * 512), dtype=np.float32)
    for k in range(3):
        m = 128 * k + np.arange(128)
        porig = 4 * (m // 3) + (m % 3)
        keep = porig[:, None] <= np.arange(512)[None, :]
        masks[:, 512 * k:512 * (k + 1)] = np.where(keep, 0.0, MASKVAL)
    return identb, identf, masks.astype(ml_dtypes.bfloat16)


_NC_CACHE = {}


def kernel(L0_attn_x, L0_attn_Wq, L0_attn_bq, L0_attn_Wk, L0_attn_bk, L0_attn_Wv,
           L0_attn_bv, L0_attn_Wp, L0_attn_bp, L0_ffn_w1, L0_ffn_b1, L0_ffn_w2,
           L0_ffn_b2):
    if "nc" not in _NC_CACHE:
        _NC_CACHE["nc"] = build_nc()
    nc = _NC_CACHE["nc"]
    identb, identf, masks = _aux_arrays()
    x = np.ascontiguousarray(np.asarray(L0_attn_x, dtype=np.float32))

    def f32(a):
        return np.ascontiguousarray(np.asarray(a, dtype=np.float32))

    common = {
        "wq": f32(L0_attn_Wq), "wk": f32(L0_attn_Wk), "wv": f32(L0_attn_Wv),
        "wp": f32(L0_attn_Wp), "bq": f32(L0_attn_bq), "bk": f32(L0_attn_bk),
        "bv": f32(L0_attn_bv), "bp": f32(L0_attn_bp), "w1": f32(L0_ffn_w1),
        "b1": f32(L0_ffn_b1), "w2": f32(L0_ffn_w2), "b2": f32(L0_ffn_b2),
        "identb": np.asarray(identb), "identf": identf, "masks": np.asarray(masks),
    }
    in_maps = [dict(common, x=np.ascontiguousarray(x[b])) for b in range(8)]
    res = run_bass_kernel_spmd(nc, in_maps, core_ids=list(range(8)))
    return np.stack([r["out"] for r in res.results], axis=0)
